# version 01 (final)
# baseline (speedup 1.0000x reference)
import numpy as np
from contextlib import ExitStack

import concourse.bass as bass
import concourse.mybir as mybir
from concourse.bass_utils import run_bass_kernel_spmd

F32 = mybir.dt.float32
U8 = mybir.dt.uint8
AF = mybir.ActivationFunctionType
ALU = mybir.AluOpType
AX = mybir.AxisListType

NCORES = 8
T = 2048
NTOK = 16384
D = 1024
NG = 4
GW = 512


class Prog:

    def __init__(self, name):
        self.nc = bass.Bass("TRN2", target_bir_lowering=False)
        self.es = ExitStack()
        self.steps = []
        self.name = name

    def din(self, name, shape, dtype=F32):
        return self.nc.dram_tensor(name, list(shape), dtype, kind="ExternalInput").ap()

    def dout(self, name, shape, dtype=F32):
        return self.nc.dram_tensor(name, list(shape), dtype, kind="ExternalOutput").ap()

    def sb(self, name, shape, dtype=F32):
        return self.es.enter_context(self.nc.sbuf_tensor(name, list(shape), dtype))

    def ps(self, name, shape, dtype=F32):
        return self.es.enter_context(self.nc.psum_tensor(name, list(shape), dtype))

    def pe(self, fn):
        self.steps.append(("tensor", fn, 1))

    def act(self, fn):
        self.steps.append(("scalar", fn, 1))

    def dve(self, fn):
        self.steps.append(("vector", fn, 1))

    def pool(self, fn):
        self.steps.append(("gpsimd", fn, 1))

    def dma(self, fn):
        self.steps.append(("sync", fn, 16))

    def finish(self):
        nc = self.nc
        sem = self.es.enter_context(nc.semaphore("gsem"))
        steps = self.steps
        total = sum(inc for _, _, inc in steps)
        with self.es:
            with nc.Block() as block:
                def section(eng_name):
                    def body(eng):
                        t = 0
                        last_pe_end = -1
                        for (e, fn, inc) in steps:
                            if e == eng_name:
                                if t > 0 and not (e == "tensor" and t == last_pe_end):
                                    eng.wait_ge(sem, t)
                                fn(eng).then_inc(sem, inc)
                                if e == "tensor":
                                    last_pe_end = t + inc
                            elif e == "tensor":
                                last_pe_end = -1
                            t += inc
                        if eng_name == "sync":
                            eng.wait_ge(sem, total)
                    return body

                block.tensor(section("tensor"))
                block.scalar(section("scalar"))
                block.vector(section("vector"))
                block.gpsimd(section("gpsimd"))
                block.sync(section("sync"))
        return self.nc


_prog_cache = {}


def cached(key, builder):
    if key not in _prog_cache:
        _prog_cache[key] = builder()
    return _prog_cache[key]


def runp(nc, in_maps):
    res = run_bass_kernel_spmd(nc, in_maps, core_ids=list(range(NCORES)))
    return res.results



def emit_ln(p, xg, lng, nkc, sq, ps_s1, ps_s2, ps_b, rows, ones_col, ones_row,
            gain=None):
    for c in range(nkc):
        xs = xg[:, c * GW:(c + 1) * GW]
        p.pe(lambda e, xs=xs, c=c: e.matmul(ps_s1[0:1, 0:GW], ones_col[:, 0:1], xs,
                                            start=(c == 0), stop=(c == nkc - 1),
                                            skip_group_check=True))
    for c in range(nkc):
        xs = xg[:, c * GW:(c + 1) * GW]
        p.act(lambda e, xs=xs: e.activation(sq[:, 0:GW], xs, AF.Square))
        p.pe(lambda e, c=c: e.matmul(ps_s2[0:1, 0:GW], ones_col[:, 0:1], sq[:, 0:GW],
                                     start=(c == 0), stop=(c == nkc - 1),
                                     skip_group_check=True))
    m = rows[0:1, 0:GW]
    ex2 = rows[0:1, GW:2 * GW]
    msq = rows[0:1, 2 * GW:3 * GW]
    var = rows[0:1, 3 * GW:4 * GW]
    alpha = rows[0:1, 4 * GW:5 * GW]
    beta = rows[0:1, 5 * GW:6 * GW]
    inv_d = 1.0 / (nkc * 128)
    p.act(lambda e: e.activation(m, ps_s1[0:1, 0:GW], AF.Copy, scale=inv_d))
    p.act(lambda e: e.activation(ex2, ps_s2[0:1, 0:GW], AF.Copy, scale=inv_d))
    p.act(lambda e: e.activation(msq, m, AF.Square))
    p.dve(lambda e: e.tensor_sub(var, ex2, msq))
    p.dve(lambda e: e.tensor_scalar_add(var, var, 1e-5))
    p.act(lambda e: e.activation(var, var, AF.Sqrt))
    p.dve(lambda e: e.reciprocal(alpha, var))
    p.dve(lambda e: e.scalar_tensor_tensor(beta, m, -1.0, alpha, ALU.mult, ALU.mult))
    p.pe(lambda e: e.matmul(ps_b[:, 0:GW], ones_row[0:1, 0:128], alpha,
                            start=True, stop=True, skip_group_check=True))
    for c in range(nkc):
        xs = xg[:, c * GW:(c + 1) * GW]
        ls = lng[:, c * GW:(c + 1) * GW]
        p.dve(lambda e, xs=xs, ls=ls: e.tensor_mul(ls, xs, ps_b[:, 0:GW]))
    p.pe(lambda e: e.matmul(ps_b[:, 0:GW], ones_row[0:1, 0:128], beta,
                            start=True, stop=True, skip_group_check=True))
    for c in range(nkc):
        ls = lng[:, c * GW:(c + 1) * GW]
        p.dve(lambda e, ls=ls: e.tensor_add(ls, ls, ps_b[:, 0:GW]))
    if gain is not None:
        for c in range(nkc):
            ls = lng[:, c * GW:(c + 1) * GW]
            p.dve(lambda e, ls=ls, c=c: e.tensor_scalar_mul(ls, ls, gain[:, c:c + 1]))



def build_dense(IF, OF, ln=False, wmode="res", wdtype="f32", actfn=None,
                scale_mode=None, scale_imm=1.0, bias=False, resid=False,
                post_l2n_obs=(), gain=False):
    p = Prog("dense")
    nkc = max(1, IF // 128)
    kpart = min(IF, 128)
    n_ob = (OF + 127) // 128
    obw_last = OF - (n_ob - 1) * 128

    x_d = p.din("x", [IF, T])
    y_d = p.dout("y", [OF, T])
    if resid:
        r_d = p.din("res", [OF, T])
    if wmode == "res":
        w_d = p.din("w", [IF, OF])
    elif wmode == "blk":
        wdt = U8 if wdtype == "u8" else F32
        w_d = p.din("w", [n_ob, 128, nkc * 128], wdt)
    elif wmode == "diff":
        w_d = p.din("w", [2 * IF, OF], U8)
    if bias:
        b_d = p.din("b", [128, n_ob])
    if scale_mode == "svec":
        s_d = p.din("s", [128, n_ob])
    elif scale_mode == "scalar":
        s_d = p.din("s", [1, 1])
    if gain:
        g_d = p.din("gain", [128, nkc])

    xg = p.sb("xg", [kpart, nkc * GW])
    lng = p.sb("lng", [kpart, nkc * GW]) if ln else xg
    sq = p.sb("sq", [128, GW])
    rows = p.sb("rows", [1, 6 * GW])
    ones_col = p.sb("ones_col", [128, 1])
    ones_row = p.sb("ones_row", [1, 128])
    ev = p.sb("ev", [128, GW])
    ev2 = p.sb("ev2", [128, GW])
    if resid:
        rg = p.sb("rg", [128, n_ob * GW])
    if bias:
        b_sb = p.sb("b_sb", [128, n_ob])
    if scale_mode == "svec":
        s_sb = p.sb("s_sb", [128, n_ob])
    elif scale_mode == "scalar":
        s_sb = p.sb("s_sb", [128, 1])
        s_in = p.sb("s_in", [1, 1])
    if gain:
        g_sb = p.sb("g_sb", [128, nkc])
    if post_l2n_obs:
        sel_c = p.sb("sel_c", [128, 2])
        sel_r = p.sb("sel_r", [2, 128])
        nr = p.sb("nr", [2, GW])

    ps_s1 = p.ps("ps_s1", [128, 512])
    ps_s2 = p.ps("ps_s2", [128, 512])
    ps_b = p.ps("ps_b", [128, 512])
    ps_mm = p.ps("ps_mm", [128, 512])
    ps_n = p.ps("ps_n", [128, 512])

    if wmode == "res":
        w_sb = p.sb("w_sb", [kpart, nkc * OF])
        for c in range(nkc):
            p.dma(lambda e, c=c: e.dma_start(out=w_sb[:, c * OF:(c + 1) * OF],
                                             in_=w_d[c * kpart:(c + 1) * kpart, :]))
    elif wmode == "blk":
        w_sb = p.sb("w_sb", [128, nkc * 128])
        if wdtype == "u8":
            wu = p.sb("wu", [128, nkc * 128], U8)
    elif wmode == "diff":
        w_sb = p.sb("w_sb", [128, nkc * OF])
        wu = p.sb("wu", [128, OF], U8)
        wu2 = p.sb("wu2", [128, OF], U8)
        for c in range(nkc):
            p.dma(lambda e, c=c: e.dma_start(out=wu[:, :], in_=w_d[c * 128:(c + 1) * 128, :]))
            p.dma(lambda e, c=c: e.dma_start(out=wu2[:, :],
                                             in_=w_d[IF + c * 128:IF + (c + 1) * 128, :]))
            p.act(lambda e, c=c: e.activation(w_sb[:, c * OF:(c + 1) * OF], wu[:, :], AF.Copy))
            p.dve(lambda e, c=c: e.tensor_sub(w_sb[:, c * OF:(c + 1) * OF],
                                              w_sb[:, c * OF:(c + 1) * OF], wu2[:, :]))

    p.dve(lambda e: e.memset(ones_col[:, :], 1.0))
    p.dve(lambda e: e.memset(ones_row[:, :], 1.0))
    if bias:
        p.dma(lambda e: e.dma_start(out=b_sb[:, :], in_=b_d[:, :]))
    if gain:
        p.dma(lambda e: e.dma_start(out=g_sb[:, :], in_=g_d[:, :]))
    if scale_mode == "svec":
        p.dma(lambda e: e.dma_start(out=s_sb[:, :], in_=s_d[:, :]))
    elif scale_mode == "scalar":
        p.dma(lambda e: e.dma_start(out=s_in[:, :], in_=s_d[:, :]))
        p.pe(lambda e: e.matmul(ps_b[:, 0:1], ones_row[0:1, 0:128], s_in[0:1, 0:1],
                                start=True, stop=True, skip_group_check=True))
        p.act(lambda e: e.activation(s_sb[:, :], ps_b[:, 0:1], AF.Copy, scale=scale_imm))
    if post_l2n_obs:
        selc_np = p.din("selc", [128, 2])
        selr_np = p.din("selr", [2, 128])
        p.dma(lambda e: e.dma_start(out=sel_c[:, :], in_=selc_np[:, :]))
        p.dma(lambda e: e.dma_start(out=sel_r[:, :], in_=selr_np[:, :]))

    act_map = {None: AF.Copy, "relu": AF.Relu, "silu": AF.Silu, "gelu": AF.Gelu,
               "sigmoid": AF.Sigmoid}

    for g in range(NG):
        for c in range(nkc):
            p.dma(lambda e, g=g, c=c: e.dma_start(
                out=xg[:, c * GW:(c + 1) * GW],
                in_=x_d[c * kpart:(c + 1) * kpart, g * GW:(g + 1) * GW]))
        if ln:
            emit_ln(p, xg, lng, nkc, sq, ps_s1, ps_s2, ps_b, rows, ones_col,
                    ones_row, gain=(g_sb if gain else None))
        if resid:
            for ob in range(n_ob):
                w = obw_last if ob == n_ob - 1 else 128
                p.dma(lambda e, g=g, ob=ob, w=w: e.dma_start(
                    out=rg[0:w, ob * GW:ob * GW + GW],
                    in_=r_d[ob * 128:ob * 128 + w, g * GW:(g + 1) * GW]))
        for ob in range(n_ob):
            obw = obw_last if ob == n_ob - 1 else 128
            if wmode == "blk":
                if wdtype == "u8":
                    p.dma(lambda e, ob=ob: e.dma_start(out=wu[:, :], in_=w_d[ob]))
                    p.act(lambda e: e.activation(w_sb[:, :], wu[:, :], AF.Copy,
                                                 scale=2.0, bias=-1.0))
                else:
                    p.dma(lambda e, ob=ob: e.dma_start(out=w_sb[:, :], in_=w_d[ob]))
            for c in range(nkc):
                if wmode == "res" or wmode == "diff":
                    lhsT = w_sb[:, c * OF + ob * 128: c * OF + ob * 128 + obw]
                else:
                    lhsT = w_sb[:, c * 128:c * 128 + obw]
                src = lng[:, c * GW:(c + 1) * GW]
                p.pe(lambda e, lhsT=lhsT, src=src, c=c: e.matmul(
                    ps_mm[0:obw, 0:GW], lhsT, src,
                    start=(c == 0), stop=(c == nkc - 1), skip_group_check=True))
            sc = 1.0
            bi = 0.0
            if scale_mode == "svec":
                sc = s_sb[:, ob:ob + 1]
            elif scale_mode == "scalar":
                sc = s_sb[:, 0:1]
            if bias:
                bi = b_sb[:, ob:ob + 1]
            if actfn == "logsig":
                p.act(lambda e, obw=obw: e.activation(ev[0:obw, :], ps_mm[0:obw, 0:GW], AF.Copy))
                p.dve(lambda e, obw=obw: e.tensor_scalar_min(ev2[0:obw, :], ev[0:obw, :], 0.0))
                p.act(lambda e, obw=obw: e.activation(ev[0:obw, :], ev[0:obw, :], AF.Abs))
                p.act(lambda e, obw=obw: e.activation(ev[0:obw, :], ev[0:obw, :], AF.Exp, scale=-1.0))
                p.act(lambda e, obw=obw: e.activation(ev[0:obw, :], ev[0:obw, :], AF.Ln, bias=1.0))
                p.dve(lambda e, obw=obw: e.tensor_sub(ev[0:obw, :], ev2[0:obw, :], ev[0:obw, :]))
            elif actfn == "negsig":
                p.act(lambda e, obw=obw, sc=sc, bi=bi: e.activation(
                    ev[0:obw, :], ps_mm[0:obw, 0:GW], AF.Sigmoid, scale=sc, bias=bi))
                p.dve(lambda e, obw=obw: e.tensor_scalar_mul(ev[0:obw, :], ev[0:obw, :], -1.0))
            elif actfn is None and bias:
                p.act(lambda e, obw=obw, sc=sc: e.activation(
                    ev[0:obw, :], ps_mm[0:obw, 0:GW], AF.Copy, scale=sc))
                p.dve(lambda e, obw=obw, bi=bi: e.tensor_scalar_add(
                    ev[0:obw, :], ev[0:obw, :], bi))
            else:
                fn = act_map[actfn]
                p.act(lambda e, obw=obw, sc=sc, bi=bi, fn=fn: e.activation(
                    ev[0:obw, :], ps_mm[0:obw, 0:GW], fn, scale=sc, bias=bi))
            if ob in post_l2n_obs:
                p.act(lambda e: e.activation(sq[:, 0:GW], ev[0:128, :], AF.Square))
                p.pe(lambda e: e.matmul(ps_n[0:2, 0:GW], sel_c[:, 0:2], sq[:, 0:GW],
                                        start=True, stop=True, skip_group_check=True))
                p.act(lambda e: e.activation(nr[:, :], ps_n[0:2, 0:GW], AF.Sqrt))
                p.dve(lambda e: e.tensor_scalar_max(nr[:, :], nr[:, :], 1e-12))
                p.dve(lambda e: e.reciprocal(nr[:, :], nr[:, :]))
                p.pe(lambda e: e.matmul(ps_n[0:128, 0:GW], sel_r[0:2, 0:128], nr[:, :],
                                        start=True, stop=True, skip_group_check=True))
                p.dve(lambda e: e.tensor_mul(ev[0:128, :], ev[0:128, :], ps_n[0:128, 0:GW]))
            if resid:
                p.dve(lambda e, ob=ob, obw=obw: e.tensor_add(
                    ev[0:obw, :], ev[0:obw, :], rg[0:obw, ob * GW:ob * GW + GW]))
            p.dma(lambda e, g=g, ob=ob, obw=obw: e.dma_start(
                out=y_d[ob * 128:ob * 128 + obw, g * GW:(g + 1) * GW],
                in_=ev[0:obw, :]))
    return p.finish()



def build_mul(F):
    p = Prog("mul")
    nkc = F // 128
    a_d = p.din("a", [F, T])
    b_d = p.din("b", [F, T])
    y_d = p.dout("y", [F, T])
    ta = p.sb("ta", [128, GW])
    tb = p.sb("tb", [128, GW])
    for g in range(NG):
        for c in range(nkc):
            p.dma(lambda e, g=g, c=c: e.dma_start(out=ta[:, :], in_=a_d[c * 128:(c + 1) * 128, g * GW:(g + 1) * GW]))
            p.dma(lambda e, g=g, c=c: e.dma_start(out=tb[:, :], in_=b_d[c * 128:(c + 1) * 128, g * GW:(g + 1) * GW]))
            p.dve(lambda e: e.tensor_mul(ta[:, :], ta[:, :], tb[:, :]))
            p.dma(lambda e, g=g, c=c: e.dma_start(out=y_d[c * 128:(c + 1) * 128, g * GW:(g + 1) * GW], in_=ta[:, :]))
    return p.finish()



def build_local():
    p = Prog("local")
    qkv_d = p.din("qkv", [3072, T])
    y_d = p.dout("y", [D, T])
    id_d = p.din("ident", [128, 64])
    qkvg = p.sb("qkvg", [128, 24 * GW])
    outg = p.sb("outg", [128, 8 * GW])
    esb = p.sb("esb", [64, 64])
    vsq = p.sb("vsq", [64, 64])
    dn = p.sb("dn", [1, 64])
    ou = p.sb("ou", [64, 64])
    ident = p.sb("identsb", [128, 64])
    ones_col = p.sb("ones_col", [128, 1])
    ones_row = p.sb("ones_row", [1, 128])
    ps_e = p.ps("ps_e", [128, 512])
    ps_d = p.ps("ps_d", [128, 512])
    ps_v = p.ps("ps_v", [128, 512])
    ps_o = p.ps("ps_o", [128, 512])
    ps_b = p.ps("ps_b", [128, 512])
    p.dma(lambda e: e.dma_start(out=ident[:, :], in_=id_d[:, :]))
    p.dve(lambda e: e.memset(ones_col[:, :], 1.0))
    p.dve(lambda e: e.memset(ones_row[:, :], 1.0))
    for g in range(NG):
        for c in range(24):
            p.dma(lambda e, g=g, c=c: e.dma_start(
                out=qkvg[:, c * GW:(c + 1) * GW],
                in_=qkv_d[c * 128:(c + 1) * 128, g * GW:(g + 1) * GW]))
        for h in range(16):
            po = (h % 2) * 64
            cq = h // 2
            ck = 8 + h // 2
            cv = 16 + h // 2
            for b in range(8):
                qs = qkvg[po:po + 64, cq * GW + b * 64: cq * GW + b * 64 + 64]
                ks = qkvg[po:po + 64, ck * GW + b * 64: ck * GW + b * 64 + 64]
                vs = qkvg[po:po + 64, cv * GW + b * 64: cv * GW + b * 64 + 64]
                p.pe(lambda e, ks=ks, qs=qs: e.matmul(ps_e[0:64, 0:64], ks, qs,
                                                      start=True, stop=True, skip_group_check=True))
                p.act(lambda e: e.activation(esb[:, :], ps_e[0:64, 0:64], AF.Exp, scale=0.125))
                p.pe(lambda e: e.matmul(ps_d[0:1, 0:64], ones_col[0:64, 0:1], esb[:, :],
                                        start=True, stop=True, skip_group_check=True))
                p.pe(lambda e, vs=vs, po=po: e.transpose(ps_v[0:64, 0:64], vs, ident[po:po + 64, 0:64]))
                p.act(lambda e: e.activation(vsq[:, :], ps_v[0:64, 0:64], AF.Copy))
                p.pe(lambda e: e.matmul(ps_o[0:64, 0:64], vsq[:, :], esb[:, :],
                                        start=True, stop=True, skip_group_check=True))
                p.act(lambda e: e.activation(dn[:, :], ps_d[0:1, 0:64], AF.Copy))
                p.dve(lambda e: e.reciprocal(dn[:, :], dn[:, :]))
                p.pe(lambda e: e.matmul(ps_b[0:64, 0:64], ones_row[0:1, 0:64], dn[:, :],
                                        start=True, stop=True, skip_group_check=True))
                p.act(lambda e: e.activation(ou[:, :], ps_o[0:64, 0:64], AF.Copy))
                p.dve(lambda e, h=h, b=b: e.tensor_mul(
                    outg[(h % 2) * 64:(h % 2) * 64 + 64,
                         (h // 2) * GW + b * 64:(h // 2) * GW + b * 64 + 64],
                    ou[:, :], ps_b[0:64, 0:64]))
        for c in range(8):
            p.dma(lambda e, g=g, c=c: e.dma_start(
                out=y_d[c * 128:(c + 1) * 128, g * GW:(g + 1) * GW],
                in_=outg[:, c * GW:(c + 1) * GW]))
    return p.finish()



def build_attn(masked, scale):
    p = Prog("attn")
    S = 8192
    ins = {}
    for u in range(2):
        for nm in ("q", "k", "v"):
            ins[(nm, u)] = p.din("%s%d" % (nm, u), [64, S])
    outs = [p.dout("o%d" % u, [64, S]) for u in range(2)]
    id_d = p.din("ident", [128, 128])
    if masked:
        dm_d = p.din("dmask", [4, 128, GW])
    kT = p.sb("kT", [64, S])
    vT = p.sb("vT", [64, S])
    qtb = p.sb("qtb", [64, GW])
    vseq = p.sb("vseq", [128, 64 * 64])
    esb = p.sb("esb", [128, GW])
    esb2 = p.sb("esb2", [128, GW])
    onest = p.sb("onest", [128, GW])
    dn = p.sb("dn", [1, GW])
    au = p.sb("au", [64, GW])
    ident = p.sb("identsb", [128, 128])
    ones_col = p.sb("ones_col", [128, 1])
    ones_row = p.sb("ones_row", [1, 128])
    if masked:
        dmask = p.sb("dmasksb", [128, 4 * GW])
    ps_s = p.ps("ps_s", [128, 512])
    ps_d = p.ps("ps_d", [128, 512])
    ps_a = p.ps("ps_a", [128, 512])
    ps_b = p.ps("ps_b", [128, 512])
    ps_t = p.ps("ps_t", [128, 512])
    p.dma(lambda e: e.dma_start(out=ident[:, :], in_=id_d[:, :]))
    p.dve(lambda e: e.memset(ones_col[:, :], 1.0))
    p.dve(lambda e: e.memset(ones_row[:, :], 1.0))
    p.dve(lambda e: e.memset(onest[:, :], 1.0))
    if masked:
        for r in range(4):
            p.dma(lambda e, r=r: e.dma_start(out=dmask[:, r * GW:(r + 1) * GW], in_=dm_d[r]))
    nqb = S // GW
    for u in range(2):
        q_d, k_d, v_d = ins[("q", u)], ins[("k", u)], ins[("v", u)]
        p.dma(lambda e, k_d=k_d: e.dma_start(out=kT[:, 0:4096], in_=k_d[:, 0:4096]))
        p.dma(lambda e, k_d=k_d: e.dma_start(out=kT[:, 4096:8192], in_=k_d[:, 4096:8192]))
        p.dma(lambda e, v_d=v_d: e.dma_start(out=vT[:, 0:4096], in_=v_d[:, 0:4096]))
        p.dma(lambda e, v_d=v_d: e.dma_start(out=vT[:, 4096:8192], in_=v_d[:, 4096:8192]))
        for j in range(64):
            p.pe(lambda e, j=j: e.transpose(ps_t[0:128, 0:64], vT[:, j * 128:(j + 1) * 128],
                                            ident[0:64, 0:64]))
            p.act(lambda e, j=j: e.activation(vseq[:, j * 64:(j + 1) * 64],
                                              ps_t[0:128, 0:64], AF.Copy))
        for qb in range(nqb):
            p.dma(lambda e, q_d=q_d, qb=qb: e.dma_start(out=qtb[:, :],
                                                        in_=q_d[:, qb * GW:(qb + 1) * GW]))
            for kt in range(64):
                if masked and kt * 128 >= (qb + 1) * GW:
                    etile = onest
                else:
                    p.pe(lambda e, kt=kt: e.matmul(ps_s[0:128, 0:GW],
                                                   kT[:, kt * 128:(kt + 1) * 128], qtb[:, :],
                                                   start=True, stop=True, skip_group_check=True))
                    p.act(lambda e, scale=scale: e.activation(esb[:, :], ps_s[0:128, 0:GW],
                                                              AF.Exp, scale=scale))
                    etile = esb
                    if masked and kt * 128 + 127 >= qb * GW:
                        r = kt - qb * 4
                        p.dve(lambda e: e.tensor_scalar_add(esb2[:, :], esb[:, :], -1.0))
                        p.dve(lambda e, r=r: e.tensor_mul(esb2[:, :], esb2[:, :],
                                                          dmask[:, r * GW:(r + 1) * GW]))
                        p.dve(lambda e: e.tensor_scalar_add(esb[:, :], esb2[:, :], 1.0))
                p.pe(lambda e, kt=kt, etile=etile: e.matmul(
                    ps_d[0:1, 0:GW], ones_col[:, 0:1], etile[:, :],
                    start=(kt == 0), stop=(kt == 63), skip_group_check=True))
                p.pe(lambda e, kt=kt, etile=etile: e.matmul(
                    ps_a[0:64, 0:GW], vseq[:, kt * 64:(kt + 1) * 64], etile[:, :],
                    start=(kt == 0), stop=(kt == 63), skip_group_check=True))
            p.act(lambda e: e.activation(dn[:, :], ps_d[0:1, 0:GW], AF.Copy))
            p.dve(lambda e: e.reciprocal(dn[:, :], dn[:, :]))
            p.pe(lambda e: e.matmul(ps_b[0:64, 0:GW], ones_row[0:1, 0:64], dn[:, :],
                                    start=True, stop=True, skip_group_check=True))
            p.act(lambda e: e.activation(au[:, :], ps_a[0:64, 0:GW], AF.Copy))
            p.dve(lambda e: e.tensor_mul(au[:, :], au[:, :], ps_b[0:64, 0:GW]))
            p.dma(lambda e, u=u, qb=qb: e.dma_start(out=outs[u][:, qb * GW:(qb + 1) * GW],
                                                    in_=au[:, :]))
    return p.finish()



def build_delta():
    p = Prog("delta")
    S = 8192
    C = 64
    NCH = S // C
    qT_d = p.din("qT", [64, S])
    kT_d = p.din("kT", [64, S])
    gkT_d = p.din("gkT", [64, S])
    gks_d = p.din("gks", [S, 64])
    vs_d = p.din("vs", [S, 64])
    gc_d = p.din("gc", [S, 1])
    triu_d = p.din("triu", [64, 64])
    sup_d = p.din("sup", [64, 64])
    til_d = p.din("til", [64, 64])
    id_d = p.din("ident", [128, 128])
    o_d = p.dout("o", [NCH, C, 64])

    sbn = lambda n, s: p.sb(n, s)
    triu = sbn("triusb", [64, 64]); supm = sbn("supm", [64, 64]); tilm = sbn("tilm", [64, 64])
    ident = sbn("identt", [128, 128])
    gks = sbn("gkssb", [64, 64]); gkT = sbn("gkTsb", [64, 64])
    qTt = sbn("qTt", [64, 64]); kTt = sbn("kTt", [64, 64]); vs = sbn("vssb", [64, 64])
    gc = sbn("gcsb", [64, 1])
    cums = sbn("cums", [64, 64])
    LT = sbn("LT", [64, 64]); e2T = sbn("e2T", [64, 64]); e3T = sbn("e3T", [64, 64])
    t3 = sbn("t3", [64, 64])
    qtT = sbn("qtT", [64, 64]); ktT = sbn("ktT", [64, 64]); abT = sbn("abT", [64, 64])
    LCc = sbn("LCc", [64, 1])
    Mb = sbn("Mb", [64, 64]); T0 = sbn("T0", [64, 64]); W0 = sbn("W0", [64, 64])
    U = sbn("U", [64, 64]); Gm = sbn("Gm", [64, 64])
    KbT = sbn("KbT", [64, 64]); Kb = sbn("Kb", [64, 64])
    S0 = sbn("S0", [64, 64])
    Osb = sbn("Osb", [64, 64])

    psA = p.ps("psA", [128, 512])
    psB = p.ps("psB", [128, 512])
    psC = p.ps("psC", [128, 512])
    psD = p.ps("psD", [128, 512])
    psE = p.ps("psE", [128, 512])
    psF = p.ps("psF", [128, 512])

    p.dma(lambda e: e.dma_start(out=triu[:, :], in_=triu_d[:, :]))
    p.dma(lambda e: e.dma_start(out=supm[:, :], in_=sup_d[:, :]))
    p.dma(lambda e: e.dma_start(out=tilm[:, :], in_=til_d[:, :]))
    p.dma(lambda e: e.dma_start(out=ident[:, :], in_=id_d[:, :]))
    p.dve(lambda e: e.memset(S0[:, :], 0.0))

    A = lambda i: i[0:64, 0:64]
    for ch in range(NCH):
        c0 = ch * C
        p.dma(lambda e, c0=c0: e.dma_start(out=gks[:, :], in_=gks_d[c0:c0 + C, :]))
        p.dma(lambda e, c0=c0: e.dma_start(out=gkT[:, :], in_=gkT_d[:, c0:c0 + C]))
        p.dma(lambda e, c0=c0: e.dma_start(out=qTt[:, :], in_=qT_d[:, c0:c0 + C]))
        p.dma(lambda e, c0=c0: e.dma_start(out=kTt[:, :], in_=kT_d[:, c0:c0 + C]))
        p.dma(lambda e, c0=c0: e.dma_start(out=vs[:, :], in_=vs_d[c0:c0 + C, :]))
        p.dma(lambda e, c0=c0: e.dma_start(out=gc[:, :], in_=gc_d[c0:c0 + C, :]))
        p.pe(lambda e: e.matmul(A(psA), triu[:, :], gks[:, :], start=True, stop=True,
                                skip_group_check=True))
        p.act(lambda e: e.activation(cums[:, :], A(psA), AF.Copy))
        p.pe(lambda e: e.transpose(A(psB), cums[:, :], ident[0:64, 0:64]))
        p.act(lambda e: e.activation(LT[:, :], A(psB), AF.Exp))
        p.act(lambda e: e.activation(e2T[:, :], A(psB), AF.Exp, scale=-1.0))
        p.dve(lambda e: e.tensor_add(t3[:, :], A(psB), gkT[:, :]))
        p.act(lambda e: e.activation(e3T[:, :], t3[:, :], AF.Exp))
        p.act(lambda e: e.activation(LCc[:, :], psB[0:64, 63:64], AF.Exp))
        p.dve(lambda e: e.tensor_mul(qtT[:, :], qTt[:, :], LT[:, :]))
        p.dve(lambda e: e.tensor_mul(ktT[:, :], kTt[:, :], e2T[:, :]))
        p.dve(lambda e: e.tensor_mul(abT[:, :], kTt[:, :], e3T[:, :]))
        p.pe(lambda e: e.matmul(A(psC), ktT[:, :], abT[:, :], start=True, stop=True,
                                skip_group_check=True))
        p.act(lambda e: e.activation(Mb[:, :], A(psC), AF.Copy))
        p.pe(lambda e: e.transpose(A(psD), Mb[:, :], ident[0:64, 0:64]))
        p.dve(lambda e: e.tensor_scalar_mul(T0[:, :], A(psD), gc[:, 0:1]))
        p.dve(lambda e: e.tensor_mul(T0[:, :], T0[:, :], supm[:, :]))
        p.pe(lambda e: e.transpose(A(psC), T0[:, :], ident[0:64, 0:64]))
        p.act(lambda e: e.activation(W0[:, :], A(psC), AF.Copy))
        p.pe(lambda e: e.matmul(A(psE), abT[:, :], S0[:, :], start=True, stop=True,
                                skip_group_check=True))
        p.dve(lambda e: e.scalar_tensor_tensor(U[:, :], A(psE), gc[:, 0:1], vs[:, :],
                                               ALU.mult, ALU.add))
        for j in range(6):
            p.pe(lambda e: e.matmul(A(psE), W0[:, :], U[:, :], start=True, stop=True,
                                    skip_group_check=True))
            p.dve(lambda e: e.tensor_add(U[:, :], U[:, :], A(psE)))
            if j < 5:
                p.pe(lambda e: e.matmul(A(psC), W0[:, :], T0[:, :], start=True, stop=True,
                                        skip_group_check=True))
                p.pe(lambda e: e.matmul(A(psD), T0[:, :], W0[:, :], start=True, stop=True,
                                        skip_group_check=True))
                p.act(lambda e: e.activation(T0[:, :], A(psC), AF.Copy))
                p.act(lambda e: e.activation(W0[:, :], A(psD), AF.Copy))
        p.pe(lambda e: e.matmul(A(psC), ktT[:, :], qtT[:, :], start=True, stop=True,
                                skip_group_check=True))
        p.dve(lambda e: e.tensor_mul(Gm[:, :], A(psC), tilm[:, :]))
        p.pe(lambda e: e.matmul(A(psF), qtT[:, :], S0[:, :], start=True, stop=False,
                                skip_group_check=True))
        p.pe(lambda e: e.matmul(A(psF), Gm[:, :], U[:, :], start=False, stop=True,
                                skip_group_check=True))
        p.act(lambda e: e.activation(Osb[:, :], A(psF), AF.Copy))
        p.dma(lambda e, ch=ch: e.dma_start(out=o_d[ch], in_=Osb[:, :]))
        p.dve(lambda e: e.tensor_scalar_mul(KbT[:, :], ktT[:, :], LCc[:, 0:1]))
        p.pe(lambda e: e.transpose(A(psC), KbT[:, :], ident[0:64, 0:64]))
        p.act(lambda e: e.activation(Kb[:, :], A(psC), AF.Copy))
        p.pe(lambda e: e.matmul(A(psD), Kb[:, :], U[:, :], start=True, stop=True,
                                skip_group_check=True))
        p.dve(lambda e: e.scalar_tensor_tensor(S0[:, :], S0[:, :], LCc[:, 0:1], A(psD),
                                               ALU.mult, ALU.add))
    return p.finish()



def shard_fm(a):
    return [np.ascontiguousarray(a[:, c * T:(c + 1) * T]) for c in range(NCORES)]


def gather_fm(results, key="y"):
    return np.concatenate([np.asarray(r[key]) for r in results], axis=1)


def blk_w(A, n_ob):
    IF, OF = A.shape
    nkc = IF // 128
    obw = OF // n_ob
    return np.ascontiguousarray(
        A.reshape(nkc, 128, n_ob, obw).transpose(2, 1, 0, 3).reshape(n_ob, 128, nkc * obw))


def colvec(b):
    OF = b.shape[0]
    n_ob = (OF + 127) // 128
    pad = np.zeros(n_ob * 128, dtype=np.float32)
    pad[:OF] = b
    return np.ascontiguousarray(pad.reshape(n_ob, 128).T)


def run_dense(key, cfg, x_fm, extra):
    nc = cached(key, lambda: build_dense(**cfg))
    xs = shard_fm(x_fm)
    in_maps = []
    for c in range(NCORES):
        m = {"x": xs[c]}
        for k, v in extra.items():
            if k == "res":
                m["res"] = np.ascontiguousarray(v[:, c * T:(c + 1) * T])
            else:
                m[k] = v
        in_maps.append(m)
    return gather_fm(runp(nc, in_maps))


IDENT = np.eye(128, dtype=np.float32)
IDENT2 = np.ascontiguousarray(np.concatenate([np.eye(64, dtype=np.float32)] * 2, axis=0))


def kernel(**inp):
    f = lambda k: np.ascontiguousarray(np.asarray(inp[k], dtype=np.float32))
    u8 = lambda k: np.ascontiguousarray(np.asarray(inp[k]).view(np.uint8))

    x = f("L0_attn_a_x")
    xfm = np.ascontiguousarray(x.reshape(NTOK, D).T)

    WqkvA = np.ascontiguousarray(f("L0_attn_a_W_qkv").T)
    qkv = run_dense("p3072b", dict(IF=D, OF=3072, ln=True, wmode="res", bias=True),
                    xfm, {"w": WqkvA, "b": colvec(f("L0_attn_a_b_qkv"))})
    nc_loc = cached("local", build_local)
    att = gather_fm(runp(nc_loc, [{"qkv": s, "ident": IDENT2} for s in shard_fm(qkv)]))
    WoutA = np.ascontiguousarray(f("L0_attn_a_W_out").T)
    xfm = run_dense("o1024br", dict(IF=D, OF=D, wmode="res", bias=True, resid=True),
                    att, {"w": WoutA, "b": colvec(f("L0_attn_a_b_out")), "res": xfm})

    k0 = u8("L0_ffn_k0")
    k1b = blk_w(u8("L0_ffn_k1"), 32)
    k2b = blk_w(u8("L0_ffn_k2"), 8)
    s0 = f("L0_ffn_s0").reshape(1, 1)
    s1 = f("L0_ffn_s1").reshape(1, 1)
    s2 = f("L0_ffn_s2").reshape(1, 1)

    def l0_ffn(xin):
        h1 = run_dense("f0a", dict(IF=D, OF=4096, ln=True, wmode="diff", wdtype="u8",
                                   actfn="relu", scale_mode="scalar", scale_imm=4.0),
                       xin, {"w": k0, "s": s0})
        h2 = run_dense("f0b", dict(IF=4096, OF=4096, wmode="blk", wdtype="u8",
                                   actfn="relu", scale_mode="scalar", scale_imm=2.0),
                       h1, {"w": k1b, "s": s1})
        return run_dense("f0c", dict(IF=4096, OF=D, wmode="blk", wdtype="u8",
                                     scale_mode="scalar", scale_imm=1.0, resid=True),
                         h2, {"w": k2b, "s": s2, "res": xin})

    xfm = l0_ffn(xfm)

    WkA = np.ascontiguousarray(f("L0_attn_b_W_key").transpose(1, 0, 2).reshape(D, D))
    WqA = np.ascontiguousarray(f("L0_attn_b_W_query").transpose(1, 0, 2).reshape(D, D))
    WvA = np.ascontiguousarray(f("L0_attn_b_W_values").transpose(1, 0, 2).reshape(D, D))
    Wcat = np.ascontiguousarray(np.concatenate([WkA, WqA, WvA], axis=1))
    bcat = np.concatenate([f("L0_attn_b_Bias_key").reshape(-1),
                           f("L0_attn_b_Bias_query").reshape(-1),
                           f("L0_attn_b_Bias_values").reshape(-1)])
    kqv = run_dense("p3072b", dict(IF=D, OF=3072, ln=True, wmode="res", bias=True),
                    xfm, {"w": Wcat, "b": colvec(bcat)})

    dmask = np.zeros((4, 128, GW), dtype=np.float32)
    for r in range(4):
        for pp in range(128):
            dmask[r, pp, :] = (np.arange(GW) >= pp + 128 * r).astype(np.float32)
    wfm = attn_pass(kqv, masked=True, scale=0.125, dmask=dmask)

    WoA = np.ascontiguousarray(f("L0_attn_b_W_output").reshape(D, D))
    xfm = run_dense("o1024br", dict(IF=D, OF=D, wmode="res", bias=True, resid=True),
                    wfm, {"w": WoA, "b": colvec(f("L0_attn_b_Bias_output")), "res": xfm})

    xfm = l0_ffn(xfm)

    Wc = np.ascontiguousarray(np.concatenate(
        [f("L1_attn_a_Wq").T, f("L1_attn_a_Wk").T, f("L1_attn_a_Wv").T], axis=1))
    qkv1 = run_dense("p3072", dict(IF=D, OF=3072, ln=True, wmode="res"),
                     xfm, {"w": Wc})
    afm = attn_pass(qkv1, masked=False, scale=1.0 / 32.0, dmask=None, order="qkv")
    xfm = run_dense("o1024br", dict(IF=D, OF=D, wmode="res", bias=True, resid=True),
                    afm, {"w": np.ascontiguousarray(f("L1_attn_a_Wu").T),
                          "b": colvec(f("L1_attn_a_bu")), "res": xfm})

    w1b = blk_w(np.ascontiguousarray(f("L1_ffn_w1_w").T), 32)
    w2b = blk_w(np.ascontiguousarray(f("L1_ffn_w2_w").T), 32)
    h1b = blk_w(np.ascontiguousarray(f("L1_ffn_h1_w").T), 32)
    h2b = blk_w(np.ascontiguousarray(f("L1_ffn_h2_w").T), 32)
    w3b = blk_w(np.ascontiguousarray(f("L1_ffn_w3_w").T), 8)
    s1v = colvec(f("L1_ffn_w1_s"))
    s2v = colvec(f("L1_ffn_w2_s"))
    h1s = colvec(f("L1_ffn_h1_s"))
    h2s = colvec(f("L1_ffn_h2_s"))
    w3s = colvec(f("L1_ffn_w3_s"))
    nc_mul4k = cached("mul4096", lambda: build_mul(4096))

    def l1_ffn(xin):
        g1 = run_dense("f1a", dict(IF=D, OF=4096, ln=True, wmode="blk",
                                   actfn="silu", scale_mode="svec"),
                       xin, {"w": w1b, "s": s1v})
        g2 = run_dense("f1b", dict(IF=D, OF=4096, ln=True, wmode="blk",
                                   scale_mode="svec"),
                       xin, {"w": w2b, "s": s2v})
        h = gather_fm(runp(nc_mul4k, [{"a": a, "b": b} for a, b in
                                      zip(shard_fm(g1), shard_fm(g2))]))
        h = run_dense("f1h", dict(IF=4096, OF=4096, wmode="blk",
                                  actfn="gelu", scale_mode="svec"),
                      h, {"w": h1b, "s": h1s})
        h = run_dense("f1h", dict(IF=4096, OF=4096, wmode="blk",
                                  actfn="gelu", scale_mode="svec"),
                      h, {"w": h2b, "s": h2s})
        return run_dense("f1o", dict(IF=4096, OF=D, wmode="blk",
                                     scale_mode="svec", resid=True),
                         h, {"w": w3b, "s": w3s, "res": xin})

    xfm = l1_ffn(xfm)

    selc = np.zeros((128, 2), dtype=np.float32)
    selc[0:64, 0] = 1.0
    selc[64:128, 1] = 1.0
    selr = np.ascontiguousarray(selc.T)
    Wqk = np.ascontiguousarray(np.concatenate([f("L1_attn_b_Wq"), f("L1_attn_b_Wk")], axis=1))
    qk = run_dense("g1qk", dict(IF=D, OF=2048, ln=True, wmode="res", actfn="silu",
                                post_l2n_obs=tuple(range(8, 16))),
                   xfm, {"w": Wqk, "selc": selc, "selr": selr})
    qfm = qk[0:1024]
    kfm = qk[1024:2048]
    vfm = run_dense("g1v", dict(IF=D, OF=D, ln=True, wmode="res"),
                    xfm, {"w": np.ascontiguousarray(f("L1_attn_b_Wv"))})
    zf = run_dense("g1f1", dict(IF=D, OF=64, ln=True, wmode="res"),
                   xfm, {"w": np.ascontiguousarray(f("L1_attn_b_Wf1"))})
    gkfm = run_dense("g1f2", dict(IF=64, OF=D, wmode="res", actfn="logsig"),
                     zf, {"w": np.ascontiguousarray(f("L1_attn_b_Wf2"))})
    gmfm = run_dense("g1gm", dict(IF=D, OF=16, ln=True, wmode="res", actfn="negsig"),
                     xfm, {"w": np.ascontiguousarray(f("L1_attn_b_Wgamma"))})
    og1 = run_dense("g1f1", dict(IF=D, OF=64, ln=True, wmode="res"),
                    xfm, {"w": np.ascontiguousarray(f("L1_attn_b_Wog1"))})
    ogfm = run_dense("g1og2", dict(IF=64, OF=D, wmode="res", actfn="sigmoid"),
                     og1, {"w": np.ascontiguousarray(f("L1_attn_b_Wog2"))})

    triu = np.triu(np.ones((64, 64), dtype=np.float32))
    supm = np.tril(np.ones((64, 64), dtype=np.float32), -1)
    tilm = np.triu(np.ones((64, 64), dtype=np.float32))
    nc_delta = cached("delta", build_delta)
    ofm = np.zeros((D, NTOK), dtype=np.float32)
    for rnd in range(4):
        in_maps = []
        units = []
        for c in range(NCORES):
            un = 4 * c + rnd
            b, h = un // 16, un % 16
            units.append((b, h))
            sl = slice(b * 8192, (b + 1) * 8192)
            qT = np.ascontiguousarray(qfm[h * 64:(h + 1) * 64, sl])
            kT = np.ascontiguousarray(kfm[h * 64:(h + 1) * 64, sl])
            gkT = np.ascontiguousarray(gkfm[h * 64:(h + 1) * 64, sl])
            in_maps.append({
                "qT": qT, "kT": kT, "gkT": gkT,
                "gks": np.ascontiguousarray(gkT.T),
                "vs": np.ascontiguousarray(vfm[h * 64:(h + 1) * 64, sl].T),
                "gc": np.ascontiguousarray(gmfm[h, sl].reshape(8192, 1)),
                "triu": triu, "sup": supm, "til": tilm, "ident": IDENT,
            })
        res = runp(nc_delta, in_maps)
        for c, (b, h) in enumerate(units):
            O = np.asarray(res[c]["o"]).reshape(8192, 64)
            ofm[h * 64:(h + 1) * 64, b * 8192:(b + 1) * 8192] = O.T

    nc_mul1k = cached("mul1024", lambda: build_mul(1024))
    o2 = gather_fm(runp(nc_mul1k, [{"a": a, "b": b} for a, b in
                                   zip(shard_fm(ofm), shard_fm(ogfm))]))
    nw = f("L1_attn_b_norm_w")
    xfm = run_dense("g2", dict(IF=D, OF=D, ln=True, gain=True, wmode="res", resid=True),
                    o2, {"w": np.ascontiguousarray(f("L1_attn_b_Wo")),
                         "gain": np.ascontiguousarray(nw.reshape(8, 128).T), "res": xfm})

    xfm = l1_ffn(xfm)

    return np.ascontiguousarray(xfm.T).reshape(2, 8192, D).astype(np.float32)


def attn_pass(qkv_fm, masked, scale, dmask, order="kqv"):
    key = "attn_m" if masked else "attn_u"
    nc = cached(key, lambda: build_attn(masked, scale))
    if order == "kqv":
        kS, qS, vS = 0, 1024, 2048
    else:
        qS, kS, vS = 0, 1024, 2048
    out = np.zeros((D, NTOK), dtype=np.float32)
    for li in range(2):
        in_maps = []
        units = []
        for c in range(NCORES):
            m = {"ident": IDENT}
            if masked:
                m["dmask"] = dmask
            uu = []
            for j in range(2):
                un = 4 * c + 2 * li + j
                b, n = un // 16, un % 16
                uu.append((b, n))
                sl = slice(b * 8192, (b + 1) * 8192)
                m["q%d" % j] = np.ascontiguousarray(qkv_fm[qS + n * 64:qS + (n + 1) * 64, sl])
                m["k%d" % j] = np.ascontiguousarray(qkv_fm[kS + n * 64:kS + (n + 1) * 64, sl])
                m["v%d" % j] = np.ascontiguousarray(qkv_fm[vS + n * 64:vS + (n + 1) * 64, sl])
            units.append(uu)
            in_maps.append(m)
        res = runp(nc, in_maps)
        for c in range(NCORES):
            for j, (b, n) in enumerate(units[c]):
                out[n * 64:(n + 1) * 64, b * 8192:(b + 1) * 8192] = np.asarray(res[c]["o%d" % j])
    return out


# version 04
# speedup vs baseline: 1.0321x; 1.0321x over previous
import numpy as np
from contextlib import ExitStack

import concourse.bass as bass
import concourse.mybir as mybir
from concourse.bass_utils import run_bass_kernel_spmd

F32 = mybir.dt.float32
U8 = mybir.dt.uint8
AF = mybir.ActivationFunctionType
ALU = mybir.AluOpType
AX = mybir.AxisListType

NCORES = 8
T = 2048
NTOK = 16384
D = 1024
NG = 4
GW = 512


class Prog:

    def __init__(self, name, mode="deps"):
        self.nc = bass.Bass("TRN2", target_bir_lowering=False)
        self.es = ExitStack()
        self.steps = []
        self.name = name
        self.mode = mode

    def din(self, name, shape, dtype=F32):
        return self.nc.dram_tensor(name, list(shape), dtype, kind="ExternalInput").ap()

    def dout(self, name, shape, dtype=F32):
        return self.nc.dram_tensor(name, list(shape), dtype, kind="ExternalOutput").ap()

    def sb(self, name, shape, dtype=F32):
        return self.es.enter_context(self.nc.sbuf_tensor(name, list(shape), dtype))

    def ps(self, name, shape, dtype=F32):
        return self.es.enter_context(self.nc.psum_tensor(name, list(shape), dtype))

    def pe(self, fn):
        self.steps.append(("tensor", fn, 1))

    def act(self, fn):
        self.steps.append(("scalar", fn, 1))

    def dve(self, fn):
        self.steps.append(("vector", fn, 1))

    def pool(self, fn):
        self.steps.append(("gpsimd", fn, 1))

    def dma(self, fn):
        self.steps.append(("sync", fn, 16))

    def finish(self):
        if self.mode == "barrier":
            return self.finish_barrier()
        return self.finish_deps()

    def finish_barrier(self):
        nc = self.nc
        sem = self.es.enter_context(nc.semaphore("gsem"))
        steps = self.steps
        total = sum(inc for _, _, inc in steps)
        with self.es:
            with nc.Block() as block:
                def section(eng_name):
                    def body(eng):
                        t = 0
                        last_pe_end = -1
                        for (e, fn, inc) in steps:
                            if e == eng_name:
                                if t > 0 and not (e == "tensor" and t == last_pe_end):
                                    eng.wait_ge(sem, t)
                                fn(eng).then_inc(sem, inc)
                                if e == "tensor":
                                    last_pe_end = t + inc
                            t += inc
                        if eng_name == "sync":
                            eng.wait_ge(sem, total)
                    return body

                block.tensor(section("tensor"))
                block.scalar(section("scalar"))
                block.vector(section("vector"))
                block.gpsimd(section("gpsimd"))
                block.sync(section("sync"))
        return self.nc

    def finish_deps(self):
        nc = self.nc
        steps = self.steps
        n = len(steps)

        class Rec:
            def __init__(self):
                self.aps = []

            def __getattr__(self, name):
                def call(*a, **k):
                    for v in list(a) + list(k.values()):
                        if isinstance(v, bass.AP):
                            self.aps.append(v)
                    return None
                return call

        def bufname(ap):
            t = getattr(ap, "tensor", None)
            return getattr(t, "name", None) or str(t)

        reads = [None] * n
        writes = [None] * n
        for i, (e, fn, inc) in enumerate(steps):
            r = Rec()
            fn(r)
            aps = r.aps
            if not aps:
                reads[i], writes[i] = set(), set()
                continue
            wb = bufname(aps[0])
            writes[i] = {wb}
            reads[i] = {bufname(a) for a in aps[1:]}
            reads[i].add(wb)

        deps = [set() for _ in range(n)]
        last_write = {}
        readers = {}
        for i in range(n):
            for b in reads[i]:
                if b in last_write:
                    deps[i].add(last_write[b])
            for b in writes[i]:
                if b in last_write:
                    deps[i].add(last_write[b])
                for rd in readers.get(b, ()):
                    deps[i].add(rd)
            for b in reads[i]:
                readers.setdefault(b, []).append(i)
            for b in writes[i]:
                last_write[b] = i
                readers[b] = []
            deps[i].discard(i)
        prev_sync = None
        for i, (e, fn, inc) in enumerate(steps):
            if e == "sync":
                if prev_sync is not None:
                    deps[i].add(prev_sync)
                prev_sync = i

        engs = ["tensor", "scalar", "vector", "gpsimd", "sync"]
        sems = {e: self.es.enter_context(nc.semaphore("sem_" + e)) for e in engs}
        eidx = [0] * n
        ecount = {e: 0 for e in engs}
        val = [0] * n
        for i, (e, fn, inc) in enumerate(steps):
            eidx[i] = ecount[e]
            ecount[e] += 1
            val[i] = (eidx[i] + 1) * (16 if inc == 16 else 1)
        etotal = {e: 0 for e in engs}
        for i, (e, fn, inc) in enumerate(steps):
            etotal[e] = val[i]

        with self.es:
            with nc.Block() as block:
                def section(eng_name):
                    def body(eng):
                        guaranteed = {e: 0 for e in engs}
                        for i, (e, fn, inc) in enumerate(steps):
                            if e != eng_name:
                                continue
                            need = {}
                            for d in deps[i]:
                                de = steps[d][0]
                                if de == "tensor" and eng_name == "tensor":
                                    continue
                                need[de] = max(need.get(de, 0), val[d])
                            for de, v in need.items():
                                if guaranteed[de] < v:
                                    eng.wait_ge(sems[de], v)
                                    guaranteed[de] = v
                            fn(eng).then_inc(sems[e], 16 if inc == 16 else 1)
                        if eng_name == "sync":
                            for de in engs:
                                if etotal[de] > 0:
                                    eng.wait_ge(sems[de], etotal[de])
                    return body

                block.tensor(section("tensor"))
                block.scalar(section("scalar"))
                block.vector(section("vector"))
                block.gpsimd(section("gpsimd"))
                block.sync(section("sync"))
        return self.nc


_prog_cache = {}


def cached(key, builder):
    if key not in _prog_cache:
        _prog_cache[key] = builder()
    return _prog_cache[key]


def runp(nc, in_maps):
    res = run_bass_kernel_spmd(nc, in_maps, core_ids=list(range(NCORES)))
    return res.results



def emit_ln(p, xg, lng, nkc, sq, ps_s1, ps_s2, ps_b, rows, ones_col, ones_row,
            gain=None):
    for c in range(nkc):
        xs = xg[:, c * GW:(c + 1) * GW]
        p.pe(lambda e, xs=xs, c=c: e.matmul(ps_s1[0:1, 0:GW], ones_col[:, 0:1], xs,
                                            start=(c == 0), stop=(c == nkc - 1),
                                            skip_group_check=True))
    for c in range(nkc):
        xs = xg[:, c * GW:(c + 1) * GW]
        p.act(lambda e, xs=xs: e.activation(sq[:, 0:GW], xs, AF.Square))
        p.pe(lambda e, c=c: e.matmul(ps_s2[0:1, 0:GW], ones_col[:, 0:1], sq[:, 0:GW],
                                     start=(c == 0), stop=(c == nkc - 1),
                                     skip_group_check=True))
    m = rows[0:1, 0:GW]
    ex2 = rows[0:1, GW:2 * GW]
    msq = rows[0:1, 2 * GW:3 * GW]
    var = rows[0:1, 3 * GW:4 * GW]
    alpha = rows[0:1, 4 * GW:5 * GW]
    beta = rows[0:1, 5 * GW:6 * GW]
    inv_d = 1.0 / (nkc * 128)
    p.act(lambda e: e.activation(m, ps_s1[0:1, 0:GW], AF.Copy, scale=inv_d))
    p.act(lambda e: e.activation(ex2, ps_s2[0:1, 0:GW], AF.Copy, scale=inv_d))
    p.act(lambda e: e.activation(msq, m, AF.Square))
    p.dve(lambda e: e.tensor_sub(var, ex2, msq))
    p.dve(lambda e: e.tensor_scalar_add(var, var, 1e-5))
    p.act(lambda e: e.activation(var, var, AF.Sqrt))
    p.dve(lambda e: e.reciprocal(alpha, var))
    p.dve(lambda e: e.scalar_tensor_tensor(beta, m, -1.0, alpha, ALU.mult, ALU.mult))
    p.pe(lambda e: e.matmul(ps_b[:, 0:GW], ones_row[0:1, 0:128], alpha,
                            start=True, stop=True, skip_group_check=True))
    for c in range(nkc):
        xs = xg[:, c * GW:(c + 1) * GW]
        ls = lng[:, c * GW:(c + 1) * GW]
        p.dve(lambda e, xs=xs, ls=ls: e.tensor_mul(ls, xs, ps_b[:, 0:GW]))
    p.pe(lambda e: e.matmul(ps_b[:, 0:GW], ones_row[0:1, 0:128], beta,
                            start=True, stop=True, skip_group_check=True))
    for c in range(nkc):
        ls = lng[:, c * GW:(c + 1) * GW]
        p.dve(lambda e, ls=ls: e.tensor_add(ls, ls, ps_b[:, 0:GW]))
    if gain is not None:
        for c in range(nkc):
            ls = lng[:, c * GW:(c + 1) * GW]
            p.dve(lambda e, ls=ls, c=c: e.tensor_scalar_mul(ls, ls, gain[:, c:c + 1]))



def build_dense(IF, OF, ln=False, wmode="res", wdtype="f32", actfn=None,
                scale_mode=None, scale_imm=1.0, bias=False, resid=False,
                post_l2n_obs=(), gain=False):
    p = Prog("dense")
    nkc = max(1, IF // 128)
    kpart = min(IF, 128)
    n_ob = (OF + 127) // 128
    obw_last = OF - (n_ob - 1) * 128

    x_d = p.din("x", [IF, T])
    y_d = p.dout("y", [OF, T])
    if resid:
        r_d = p.din("res", [OF, T])
    if wmode == "res":
        w_d = p.din("w", [IF, OF])
    elif wmode == "blk":
        wdt = U8 if wdtype == "u8" else F32
        w_d = p.din("w", [n_ob, 128, nkc * 128], wdt)
    elif wmode == "diff":
        w_d = p.din("w", [2 * IF, OF], U8)
    if bias:
        b_d = p.din("b", [128, n_ob])
    if scale_mode == "svec":
        s_d = p.din("s", [128, n_ob])
    elif scale_mode == "scalar":
        s_d = p.din("s", [1, 1])
    if gain:
        g_d = p.din("gain", [128, nkc])

    xg = p.sb("xg", [kpart, nkc * GW])
    lng = p.sb("lng", [kpart, nkc * GW]) if ln else xg
    sq = p.sb("sq", [128, GW])
    rows = p.sb("rows", [1, 6 * GW])
    ones_col = p.sb("ones_col", [128, 1])
    ones_row = p.sb("ones_row", [1, 128])
    ev = p.sb("ev", [128, GW])
    ev2 = p.sb("ev2", [128, GW])
    if resid:
        rg = p.sb("rg", [128, n_ob * GW])
    if bias:
        b_sb = p.sb("b_sb", [128, n_ob])
    if scale_mode == "svec":
        s_sb = p.sb("s_sb", [128, n_ob])
    elif scale_mode == "scalar":
        s_sb = p.sb("s_sb", [128, 1])
        s_in = p.sb("s_in", [1, 1])
    if gain:
        g_sb = p.sb("g_sb", [128, nkc])
    if post_l2n_obs:
        sel_c = p.sb("sel_c", [128, 2])
        sel_r = p.sb("sel_r", [2, 128])
        nr = p.sb("nr", [2, GW])

    ps_s1 = p.ps("ps_s1", [128, 512])
    ps_s2 = p.ps("ps_s2", [128, 512])
    ps_b = p.ps("ps_b", [128, 512])
    ps_mm = p.ps("ps_mm", [128, 512])
    ps_n = p.ps("ps_n", [128, 512])

    if wmode == "res":
        w_sb = p.sb("w_sb", [kpart, nkc * OF])
        for c in range(nkc):
            p.dma(lambda e, c=c: e.dma_start(out=w_sb[:, c * OF:(c + 1) * OF],
                                             in_=w_d[c * kpart:(c + 1) * kpart, :]))
    elif wmode == "blk":
        w_sb = p.sb("w_sb", [128, nkc * 128])
        if wdtype == "u8":
            wu = p.sb("wu", [128, nkc * 128], U8)
    elif wmode == "diff":
        w_sb = p.sb("w_sb", [128, nkc * OF])
        wu = p.sb("wu", [128, OF], U8)
        wu2 = p.sb("wu2", [128, OF], U8)
        for c in range(nkc):
            p.dma(lambda e, c=c: e.dma_start(out=wu[:, :], in_=w_d[c * 128:(c + 1) * 128, :]))
            p.dma(lambda e, c=c: e.dma_start(out=wu2[:, :],
                                             in_=w_d[IF + c * 128:IF + (c + 1) * 128, :]))
            p.act(lambda e, c=c: e.activation(w_sb[:, c * OF:(c + 1) * OF], wu[:, :], AF.Copy))
            p.dve(lambda e, c=c: e.tensor_sub(w_sb[:, c * OF:(c + 1) * OF],
                                              w_sb[:, c * OF:(c + 1) * OF], wu2[:, :]))

    p.dve(lambda e: e.memset(ones_col[:, :], 1.0))
    p.dve(lambda e: e.memset(ones_row[:, :], 1.0))
    if bias:
        p.dma(lambda e: e.dma_start(out=b_sb[:, :], in_=b_d[:, :]))
    if gain:
        p.dma(lambda e: e.dma_start(out=g_sb[:, :], in_=g_d[:, :]))
    if scale_mode == "svec":
        p.dma(lambda e: e.dma_start(out=s_sb[:, :], in_=s_d[:, :]))
    elif scale_mode == "scalar":
        p.dma(lambda e: e.dma_start(out=s_in[:, :], in_=s_d[:, :]))
        p.pe(lambda e: e.matmul(ps_b[:, 0:1], ones_row[0:1, 0:128], s_in[0:1, 0:1],
                                start=True, stop=True, skip_group_check=True))
        p.act(lambda e: e.activation(s_sb[:, :], ps_b[:, 0:1], AF.Copy, scale=scale_imm))
    if post_l2n_obs:
        selc_np = p.din("selc", [128, 2])
        selr_np = p.din("selr", [2, 128])
        p.dma(lambda e: e.dma_start(out=sel_c[:, :], in_=selc_np[:, :]))
        p.dma(lambda e: e.dma_start(out=sel_r[:, :], in_=selr_np[:, :]))

    act_map = {None: AF.Copy, "relu": AF.Relu, "silu": AF.Silu, "gelu": AF.Gelu,
               "sigmoid": AF.Sigmoid}

    for g in range(NG):
        for c in range(nkc):
            p.dma(lambda e, g=g, c=c: e.dma_start(
                out=xg[:, c * GW:(c + 1) * GW],
                in_=x_d[c * kpart:(c + 1) * kpart, g * GW:(g + 1) * GW]))
        if ln:
            emit_ln(p, xg, lng, nkc, sq, ps_s1, ps_s2, ps_b, rows, ones_col,
                    ones_row, gain=(g_sb if gain else None))
        if resid:
            for ob in range(n_ob):
                w = obw_last if ob == n_ob - 1 else 128
                p.dma(lambda e, g=g, ob=ob, w=w: e.dma_start(
                    out=rg[0:w, ob * GW:ob * GW + GW],
                    in_=r_d[ob * 128:ob * 128 + w, g * GW:(g + 1) * GW]))
        for ob in range(n_ob):
            obw = obw_last if ob == n_ob - 1 else 128
            if wmode == "blk":
                if wdtype == "u8":
                    p.dma(lambda e, ob=ob: e.dma_start(out=wu[:, :], in_=w_d[ob]))
                    p.act(lambda e: e.activation(w_sb[:, :], wu[:, :], AF.Copy,
                                                 scale=2.0, bias=-1.0))
                else:
                    p.dma(lambda e, ob=ob: e.dma_start(out=w_sb[:, :], in_=w_d[ob]))
            for c in range(nkc):
                if wmode == "res" or wmode == "diff":
                    lhsT = w_sb[:, c * OF + ob * 128: c * OF + ob * 128 + obw]
                else:
                    lhsT = w_sb[:, c * 128:c * 128 + obw]
                src = lng[:, c * GW:(c + 1) * GW]
                p.pe(lambda e, lhsT=lhsT, src=src, c=c: e.matmul(
                    ps_mm[0:obw, 0:GW], lhsT, src,
                    start=(c == 0), stop=(c == nkc - 1), skip_group_check=True))
            sc = 1.0
            bi = 0.0
            if scale_mode == "svec":
                sc = s_sb[:, ob:ob + 1]
            elif scale_mode == "scalar":
                sc = s_sb[:, 0:1]
            if bias:
                bi = b_sb[:, ob:ob + 1]
            if actfn == "logsig":
                p.act(lambda e, obw=obw: e.activation(ev[0:obw, :], ps_mm[0:obw, 0:GW], AF.Copy))
                p.dve(lambda e, obw=obw: e.tensor_scalar_min(ev2[0:obw, :], ev[0:obw, :], 0.0))
                p.act(lambda e, obw=obw: e.activation(ev[0:obw, :], ev[0:obw, :], AF.Abs))
                p.act(lambda e, obw=obw: e.activation(ev[0:obw, :], ev[0:obw, :], AF.Exp, scale=-1.0))
                p.act(lambda e, obw=obw: e.activation(ev[0:obw, :], ev[0:obw, :], AF.Ln, bias=1.0))
                p.dve(lambda e, obw=obw: e.tensor_sub(ev[0:obw, :], ev2[0:obw, :], ev[0:obw, :]))
            elif actfn == "negsig":
                p.act(lambda e, obw=obw, sc=sc, bi=bi: e.activation(
                    ev[0:obw, :], ps_mm[0:obw, 0:GW], AF.Sigmoid, scale=sc, bias=bi))
                p.dve(lambda e, obw=obw: e.tensor_scalar_mul(ev[0:obw, :], ev[0:obw, :], -1.0))
            elif actfn is None and bias:
                p.act(lambda e, obw=obw, sc=sc: e.activation(
                    ev[0:obw, :], ps_mm[0:obw, 0:GW], AF.Copy, scale=sc))
                p.dve(lambda e, obw=obw, bi=bi: e.tensor_scalar_add(
                    ev[0:obw, :], ev[0:obw, :], bi))
            else:
                fn = act_map[actfn]
                p.act(lambda e, obw=obw, sc=sc, bi=bi, fn=fn: e.activation(
                    ev[0:obw, :], ps_mm[0:obw, 0:GW], fn, scale=sc, bias=bi))
            if ob in post_l2n_obs:
                p.act(lambda e: e.activation(sq[:, 0:GW], ev[0:128, :], AF.Square))
                p.pe(lambda e: e.matmul(ps_n[0:2, 0:GW], sel_c[:, 0:2], sq[:, 0:GW],
                                        start=True, stop=True, skip_group_check=True))
                p.act(lambda e: e.activation(nr[:, :], ps_n[0:2, 0:GW], AF.Sqrt))
                p.dve(lambda e: e.tensor_scalar_max(nr[:, :], nr[:, :], 1e-12))
                p.dve(lambda e: e.reciprocal(nr[:, :], nr[:, :]))
                p.pe(lambda e: e.matmul(ps_n[0:128, 0:GW], sel_r[0:2, 0:128], nr[:, :],
                                        start=True, stop=True, skip_group_check=True))
                p.dve(lambda e: e.tensor_mul(ev[0:128, :], ev[0:128, :], ps_n[0:128, 0:GW]))
            if resid:
                p.dve(lambda e, ob=ob, obw=obw: e.tensor_add(
                    ev[0:obw, :], ev[0:obw, :], rg[0:obw, ob * GW:ob * GW + GW]))
            p.dma(lambda e, g=g, ob=ob, obw=obw: e.dma_start(
                out=y_d[ob * 128:ob * 128 + obw, g * GW:(g + 1) * GW],
                in_=ev[0:obw, :]))
    return p.finish()



def build_mul(F):
    p = Prog("mul")
    nkc = F // 128
    a_d = p.din("a", [F, T])
    b_d = p.din("b", [F, T])
    y_d = p.dout("y", [F, T])
    ta = p.sb("ta", [128, GW])
    tb = p.sb("tb", [128, GW])
    for g in range(NG):
        for c in range(nkc):
            p.dma(lambda e, g=g, c=c: e.dma_start(out=ta[:, :], in_=a_d[c * 128:(c + 1) * 128, g * GW:(g + 1) * GW]))
            p.dma(lambda e, g=g, c=c: e.dma_start(out=tb[:, :], in_=b_d[c * 128:(c + 1) * 128, g * GW:(g + 1) * GW]))
            p.dve(lambda e: e.tensor_mul(ta[:, :], ta[:, :], tb[:, :]))
            p.dma(lambda e, g=g, c=c: e.dma_start(out=y_d[c * 128:(c + 1) * 128, g * GW:(g + 1) * GW], in_=ta[:, :]))
    return p.finish()



def build_local():
    p = Prog("local")
    qkv_d = p.din("qkv", [3072, T])
    y_d = p.dout("y", [D, T])
    id_d = p.din("ident", [128, 64])
    qkvg = p.sb("qkvg", [128, 24 * GW])
    outg = p.sb("outg", [128, 8 * GW])
    esb = p.sb("esb", [64, 64])
    vsq = p.sb("vsq", [64, 64])
    dn = p.sb("dn", [1, 64])
    ou = p.sb("ou", [64, 64])
    ident = p.sb("identsb", [128, 64])
    ones_col = p.sb("ones_col", [128, 1])
    ones_row = p.sb("ones_row", [1, 128])
    ps_e = p.ps("ps_e", [128, 512])
    ps_d = p.ps("ps_d", [128, 512])
    ps_v = p.ps("ps_v", [128, 512])
    ps_o = p.ps("ps_o", [128, 512])
    ps_b = p.ps("ps_b", [128, 512])
    p.dma(lambda e: e.dma_start(out=ident[:, :], in_=id_d[:, :]))
    p.dve(lambda e: e.memset(ones_col[:, :], 1.0))
    p.dve(lambda e: e.memset(ones_row[:, :], 1.0))
    for g in range(NG):
        for c in range(24):
            p.dma(lambda e, g=g, c=c: e.dma_start(
                out=qkvg[:, c * GW:(c + 1) * GW],
                in_=qkv_d[c * 128:(c + 1) * 128, g * GW:(g + 1) * GW]))
        for h in range(16):
            po = (h % 2) * 64
            cq = h // 2
            ck = 8 + h // 2
            cv = 16 + h // 2
            for b in range(8):
                qs = qkvg[po:po + 64, cq * GW + b * 64: cq * GW + b * 64 + 64]
                ks = qkvg[po:po + 64, ck * GW + b * 64: ck * GW + b * 64 + 64]
                vs = qkvg[po:po + 64, cv * GW + b * 64: cv * GW + b * 64 + 64]
                p.pe(lambda e, ks=ks, qs=qs: e.matmul(ps_e[0:64, 0:64], ks, qs,
                                                      start=True, stop=True, skip_group_check=True))
                p.act(lambda e: e.activation(esb[:, :], ps_e[0:64, 0:64], AF.Exp, scale=0.125))
                p.pe(lambda e: e.matmul(ps_d[0:1, 0:64], ones_col[0:64, 0:1], esb[:, :],
                                        start=True, stop=True, skip_group_check=True))
                p.pe(lambda e, vs=vs, po=po: e.transpose(ps_v[0:64, 0:64], vs, ident[po:po + 64, 0:64]))
                p.act(lambda e: e.activation(vsq[:, :], ps_v[0:64, 0:64], AF.Copy))
                p.pe(lambda e: e.matmul(ps_o[0:64, 0:64], vsq[:, :], esb[:, :],
                                        start=True, stop=True, skip_group_check=True))
                p.act(lambda e: e.activation(dn[:, :], ps_d[0:1, 0:64], AF.Copy))
                p.dve(lambda e: e.reciprocal(dn[:, :], dn[:, :]))
                p.pe(lambda e: e.matmul(ps_b[0:64, 0:64], ones_row[0:1, 0:64], dn[:, :],
                                        start=True, stop=True, skip_group_check=True))
                p.act(lambda e: e.activation(ou[:, :], ps_o[0:64, 0:64], AF.Copy))
                p.dve(lambda e, h=h, b=b: e.tensor_mul(
                    outg[(h % 2) * 64:(h % 2) * 64 + 64,
                         (h // 2) * GW + b * 64:(h // 2) * GW + b * 64 + 64],
                    ou[:, :], ps_b[0:64, 0:64]))
        for c in range(8):
            p.dma(lambda e, g=g, c=c: e.dma_start(
                out=y_d[c * 128:(c + 1) * 128, g * GW:(g + 1) * GW],
                in_=outg[:, c * GW:(c + 1) * GW]))
    return p.finish()



def build_attn(masked, scale):
    p = Prog("attn")
    S = 8192
    ins = {}
    for u in range(2):
        for nm in ("q", "k", "v"):
            ins[(nm, u)] = p.din("%s%d" % (nm, u), [64, S])
    outs = [p.dout("o%d" % u, [64, S]) for u in range(2)]
    id_d = p.din("ident", [128, 128])
    if masked:
        dm_d = p.din("dmask", [4, 128, GW])
    kT = p.sb("kT", [64, S])
    vT = p.sb("vT", [64, S])
    qtb = p.sb("qtb", [64, GW])
    vseq = p.sb("vseq", [128, 64 * 64])
    esb = p.sb("esb", [128, GW])
    esb2 = p.sb("esb2", [128, GW])
    onest = p.sb("onest", [128, GW])
    dn = p.sb("dn", [1, GW])
    au = p.sb("au", [64, GW])
    ident = p.sb("identsb", [128, 128])
    ones_col = p.sb("ones_col", [128, 1])
    ones_row = p.sb("ones_row", [1, 128])
    if masked:
        dmask = p.sb("dmasksb", [128, 4 * GW])
    ps_s = p.ps("ps_s", [128, 512])
    ps_d = p.ps("ps_d", [128, 512])
    ps_a = p.ps("ps_a", [128, 512])
    ps_b = p.ps("ps_b", [128, 512])
    ps_t = p.ps("ps_t", [128, 512])
    p.dma(lambda e: e.dma_start(out=ident[:, :], in_=id_d[:, :]))
    p.dve(lambda e: e.memset(ones_col[:, :], 1.0))
    p.dve(lambda e: e.memset(ones_row[:, :], 1.0))
    p.dve(lambda e: e.memset(onest[:, :], 1.0))
    if masked:
        for r in range(4):
            p.dma(lambda e, r=r: e.dma_start(out=dmask[:, r * GW:(r + 1) * GW], in_=dm_d[r]))
    nqb = S // GW
    for u in range(2):
        q_d, k_d, v_d = ins[("q", u)], ins[("k", u)], ins[("v", u)]
        p.dma(lambda e, k_d=k_d: e.dma_start(out=kT[:, 0:4096], in_=k_d[:, 0:4096]))
        p.dma(lambda e, k_d=k_d: e.dma_start(out=kT[:, 4096:8192], in_=k_d[:, 4096:8192]))
        p.dma(lambda e, v_d=v_d: e.dma_start(out=vT[:, 0:4096], in_=v_d[:, 0:4096]))
        p.dma(lambda e, v_d=v_d: e.dma_start(out=vT[:, 4096:8192], in_=v_d[:, 4096:8192]))
        for j in range(64):
            p.pe(lambda e, j=j: e.transpose(ps_t[0:128, 0:64], vT[:, j * 128:(j + 1) * 128],
                                            ident[0:64, 0:64]))
            p.act(lambda e, j=j: e.activation(vseq[:, j * 64:(j + 1) * 64],
                                              ps_t[0:128, 0:64], AF.Copy))
        for qb in range(nqb):
            p.dma(lambda e, q_d=q_d, qb=qb: e.dma_start(out=qtb[:, :],
                                                        in_=q_d[:, qb * GW:(qb + 1) * GW]))
            for kt in range(64):
                if masked and kt * 128 >= (qb + 1) * GW:
                    etile = onest
                else:
                    p.pe(lambda e, kt=kt: e.matmul(ps_s[0:128, 0:GW],
                                                   kT[:, kt * 128:(kt + 1) * 128], qtb[:, :],
                                                   start=True, stop=True, skip_group_check=True))
                    p.act(lambda e, scale=scale: e.activation(esb[:, :], ps_s[0:128, 0:GW],
                                                              AF.Exp, scale=scale))
                    etile = esb
                    if masked and kt * 128 + 127 >= qb * GW:
                        r = kt - qb * 4
                        p.dve(lambda e: e.tensor_scalar_add(esb2[:, :], esb[:, :], -1.0))
                        p.dve(lambda e, r=r: e.tensor_mul(esb2[:, :], esb2[:, :],
                                                          dmask[:, r * GW:(r + 1) * GW]))
                        p.dve(lambda e: e.tensor_scalar_add(esb[:, :], esb2[:, :], 1.0))
                p.pe(lambda e, kt=kt, etile=etile: e.matmul(
                    ps_d[0:1, 0:GW], ones_col[:, 0:1], etile[:, :],
                    start=(kt == 0), stop=(kt == 63), skip_group_check=True))
                p.pe(lambda e, kt=kt, etile=etile: e.matmul(
                    ps_a[0:64, 0:GW], vseq[:, kt * 64:(kt + 1) * 64], etile[:, :],
                    start=(kt == 0), stop=(kt == 63), skip_group_check=True))
            p.act(lambda e: e.activation(dn[:, :], ps_d[0:1, 0:GW], AF.Copy))
            p.dve(lambda e: e.reciprocal(dn[:, :], dn[:, :]))
            p.pe(lambda e: e.matmul(ps_b[0:64, 0:GW], ones_row[0:1, 0:64], dn[:, :],
                                    start=True, stop=True, skip_group_check=True))
            p.act(lambda e: e.activation(au[:, :], ps_a[0:64, 0:GW], AF.Copy))
            p.dve(lambda e: e.tensor_mul(au[:, :], au[:, :], ps_b[0:64, 0:GW]))
            p.dma(lambda e, u=u, qb=qb: e.dma_start(out=outs[u][:, qb * GW:(qb + 1) * GW],
                                                    in_=au[:, :]))
    return p.finish()



def build_delta():
    p = Prog("delta", mode="barrier")
    S = 8192
    C = 64
    NCH = S // C
    qT_d = p.din("qT", [64, S])
    kT_d = p.din("kT", [64, S])
    gkT_d = p.din("gkT", [64, S])
    gks_d = p.din("gks", [S, 64])
    vs_d = p.din("vs", [S, 64])
    gc_d = p.din("gc", [S, 1])
    triu_d = p.din("triu", [64, 64])
    sup_d = p.din("sup", [64, 64])
    til_d = p.din("til", [64, 64])
    id_d = p.din("ident", [128, 128])
    o_d = p.dout("o", [NCH, C, 64])

    sbn = lambda n, s: p.sb(n, s)
    triu = sbn("triusb", [64, 64]); supm = sbn("supm", [64, 64]); tilm = sbn("tilm", [64, 64])
    ident = sbn("identt", [128, 128])
    gks = sbn("gkssb", [64, 64]); gkT = sbn("gkTsb", [64, 64])
    qTt = sbn("qTt", [64, 64]); kTt = sbn("kTt", [64, 64]); vs = sbn("vssb", [64, 64])
    gc = sbn("gcsb", [64, 1])
    cums = sbn("cums", [64, 64])
    LT = sbn("LT", [64, 64]); e2T = sbn("e2T", [64, 64]); e3T = sbn("e3T", [64, 64])
    t3 = sbn("t3", [64, 64])
    qtT = sbn("qtT", [64, 64]); ktT = sbn("ktT", [64, 64]); abT = sbn("abT", [64, 64])
    LCc = sbn("LCc", [64, 1])
    Mb = sbn("Mb", [64, 64]); T0 = sbn("T0", [64, 64]); W0 = sbn("W0", [64, 64])
    U = sbn("U", [64, 64]); Gm = sbn("Gm", [64, 64])
    KbT = sbn("KbT", [64, 64]); Kb = sbn("Kb", [64, 64])
    S0 = sbn("S0", [64, 64])
    Osb = sbn("Osb", [64, 64])

    psA = p.ps("psA", [128, 512])
    psB = p.ps("psB", [128, 512])
    psC = p.ps("psC", [128, 512])
    psD = p.ps("psD", [128, 512])
    psE = p.ps("psE", [128, 512])
    psF = p.ps("psF", [128, 512])

    p.dma(lambda e: e.dma_start(out=triu[:, :], in_=triu_d[:, :]))
    p.dma(lambda e: e.dma_start(out=supm[:, :], in_=sup_d[:, :]))
    p.dma(lambda e: e.dma_start(out=tilm[:, :], in_=til_d[:, :]))
    p.dma(lambda e: e.dma_start(out=ident[:, :], in_=id_d[:, :]))
    p.dve(lambda e: e.memset(S0[:, :], 0.0))

    A = lambda i: i[0:64, 0:64]
    for ch in range(NCH):
        c0 = ch * C
        p.dma(lambda e, c0=c0: e.dma_start(out=gks[:, :], in_=gks_d[c0:c0 + C, :]))
        p.dma(lambda e, c0=c0: e.dma_start(out=gkT[:, :], in_=gkT_d[:, c0:c0 + C]))
        p.dma(lambda e, c0=c0: e.dma_start(out=qTt[:, :], in_=qT_d[:, c0:c0 + C]))
        p.dma(lambda e, c0=c0: e.dma_start(out=kTt[:, :], in_=kT_d[:, c0:c0 + C]))
        p.dma(lambda e, c0=c0: e.dma_start(out=vs[:, :], in_=vs_d[c0:c0 + C, :]))
        p.dma(lambda e, c0=c0: e.dma_start(out=gc[:, :], in_=gc_d[c0:c0 + C, :]))
        p.pe(lambda e: e.matmul(A(psA), triu[:, :], gks[:, :], start=True, stop=True,
                                skip_group_check=True))
        p.act(lambda e: e.activation(cums[:, :], A(psA), AF.Copy))
        p.pe(lambda e: e.transpose(A(psB), cums[:, :], ident[0:64, 0:64]))
        p.act(lambda e: e.activation(LT[:, :], A(psB), AF.Exp))
        p.act(lambda e: e.activation(e2T[:, :], A(psB), AF.Exp, scale=-1.0))
        p.dve(lambda e: e.tensor_add(t3[:, :], A(psB), gkT[:, :]))
        p.act(lambda e: e.activation(e3T[:, :], t3[:, :], AF.Exp))
        p.act(lambda e: e.activation(LCc[:, :], psB[0:64, 63:64], AF.Exp))
        p.dve(lambda e: e.tensor_mul(qtT[:, :], qTt[:, :], LT[:, :]))
        p.dve(lambda e: e.tensor_mul(ktT[:, :], kTt[:, :], e2T[:, :]))
        p.dve(lambda e: e.tensor_mul(abT[:, :], kTt[:, :], e3T[:, :]))
        p.pe(lambda e: e.matmul(A(psC), ktT[:, :], abT[:, :], start=True, stop=True,
                                skip_group_check=True))
        p.act(lambda e: e.activation(Mb[:, :], A(psC), AF.Copy))
        p.pe(lambda e: e.transpose(A(psD), Mb[:, :], ident[0:64, 0:64]))
        p.dve(lambda e: e.tensor_scalar_mul(T0[:, :], A(psD), gc[:, 0:1]))
        p.dve(lambda e: e.tensor_mul(T0[:, :], T0[:, :], supm[:, :]))
        p.pe(lambda e: e.transpose(A(psC), T0[:, :], ident[0:64, 0:64]))
        p.act(lambda e: e.activation(W0[:, :], A(psC), AF.Copy))
        p.pe(lambda e: e.matmul(A(psE), abT[:, :], S0[:, :], start=True, stop=True,
                                skip_group_check=True))
        p.dve(lambda e: e.scalar_tensor_tensor(U[:, :], A(psE), gc[:, 0:1], vs[:, :],
                                               ALU.mult, ALU.add))
        for j in range(6):
            p.pe(lambda e: e.matmul(A(psE), W0[:, :], U[:, :], start=True, stop=True,
                                    skip_group_check=True))
            p.dve(lambda e: e.tensor_add(U[:, :], U[:, :], A(psE)))
            if j < 5:
                p.pe(lambda e: e.matmul(A(psC), W0[:, :], T0[:, :], start=True, stop=True,
                                        skip_group_check=True))
                p.pe(lambda e: e.matmul(A(psD), T0[:, :], W0[:, :], start=True, stop=True,
                                        skip_group_check=True))
                p.act(lambda e: e.activation(T0[:, :], A(psC), AF.Copy))
                p.act(lambda e: e.activation(W0[:, :], A(psD), AF.Copy))
        p.pe(lambda e: e.matmul(A(psC), ktT[:, :], qtT[:, :], start=True, stop=True,
                                skip_group_check=True))
        p.dve(lambda e: e.tensor_mul(Gm[:, :], A(psC), tilm[:, :]))
        p.pe(lambda e: e.matmul(A(psF), qtT[:, :], S0[:, :], start=True, stop=False,
                                skip_group_check=True))
        p.pe(lambda e: e.matmul(A(psF), Gm[:, :], U[:, :], start=False, stop=True,
                                skip_group_check=True))
        p.act(lambda e: e.activation(Osb[:, :], A(psF), AF.Copy))
        p.dma(lambda e, ch=ch: e.dma_start(out=o_d[ch], in_=Osb[:, :]))
        p.dve(lambda e: e.tensor_scalar_mul(KbT[:, :], ktT[:, :], LCc[:, 0:1]))
        p.pe(lambda e: e.transpose(A(psC), KbT[:, :], ident[0:64, 0:64]))
        p.act(lambda e: e.activation(Kb[:, :], A(psC), AF.Copy))
        p.pe(lambda e: e.matmul(A(psD), Kb[:, :], U[:, :], start=True, stop=True,
                                skip_group_check=True))
        p.dve(lambda e: e.scalar_tensor_tensor(S0[:, :], S0[:, :], LCc[:, 0:1], A(psD),
                                               ALU.mult, ALU.add))
    return p.finish()



def shard_fm(a):
    return [np.ascontiguousarray(a[:, c * T:(c + 1) * T]) for c in range(NCORES)]


def gather_fm(results, key="y"):
    return np.concatenate([np.asarray(r[key]) for r in results], axis=1)


def blk_w(A, n_ob):
    IF, OF = A.shape
    nkc = IF // 128
    obw = OF // n_ob
    return np.ascontiguousarray(
        A.reshape(nkc, 128, n_ob, obw).transpose(2, 1, 0, 3).reshape(n_ob, 128, nkc * obw))


def colvec(b):
    OF = b.shape[0]
    n_ob = (OF + 127) // 128
    pad = np.zeros(n_ob * 128, dtype=np.float32)
    pad[:OF] = b
    return np.ascontiguousarray(pad.reshape(n_ob, 128).T)


def run_dense(key, cfg, x_fm, extra):
    nc = cached(key, lambda: build_dense(**cfg))
    xs = shard_fm(x_fm)
    in_maps = []
    for c in range(NCORES):
        m = {"x": xs[c]}
        for k, v in extra.items():
            if k == "res":
                m["res"] = np.ascontiguousarray(v[:, c * T:(c + 1) * T])
            else:
                m[k] = v
        in_maps.append(m)
    return gather_fm(runp(nc, in_maps))


IDENT = np.eye(128, dtype=np.float32)
IDENT2 = np.ascontiguousarray(np.concatenate([np.eye(64, dtype=np.float32)] * 2, axis=0))


def kernel(**inp):
    f = lambda k: np.ascontiguousarray(np.asarray(inp[k], dtype=np.float32))
    u8 = lambda k: np.ascontiguousarray(np.asarray(inp[k]).view(np.uint8))

    x = f("L0_attn_a_x")
    xfm = np.ascontiguousarray(x.reshape(NTOK, D).T)

    WqkvA = np.ascontiguousarray(f("L0_attn_a_W_qkv").T)
    qkv = run_dense("p3072b", dict(IF=D, OF=3072, ln=True, wmode="res", bias=True),
                    xfm, {"w": WqkvA, "b": colvec(f("L0_attn_a_b_qkv"))})
    nc_loc = cached("local", build_local)
    att = gather_fm(runp(nc_loc, [{"qkv": s, "ident": IDENT2} for s in shard_fm(qkv)]))
    WoutA = np.ascontiguousarray(f("L0_attn_a_W_out").T)
    xfm = run_dense("o1024br", dict(IF=D, OF=D, wmode="res", bias=True, resid=True),
                    att, {"w": WoutA, "b": colvec(f("L0_attn_a_b_out")), "res": xfm})

    k0 = u8("L0_ffn_k0")
    k1b = blk_w(u8("L0_ffn_k1"), 32)
    k2b = blk_w(u8("L0_ffn_k2"), 8)
    s0 = f("L0_ffn_s0").reshape(1, 1)
    s1 = f("L0_ffn_s1").reshape(1, 1)
    s2 = f("L0_ffn_s2").reshape(1, 1)

    def l0_ffn(xin):
        h1 = run_dense("f0a", dict(IF=D, OF=4096, ln=True, wmode="diff", wdtype="u8",
                                   actfn="relu", scale_mode="scalar", scale_imm=4.0),
                       xin, {"w": k0, "s": s0})
        h2 = run_dense("f0b", dict(IF=4096, OF=4096, wmode="blk", wdtype="u8",
                                   actfn="relu", scale_mode="scalar", scale_imm=2.0),
                       h1, {"w": k1b, "s": s1})
        return run_dense("f0c", dict(IF=4096, OF=D, wmode="blk", wdtype="u8",
                                     scale_mode="scalar", scale_imm=1.0, resid=True),
                         h2, {"w": k2b, "s": s2, "res": xin})

    xfm = l0_ffn(xfm)

    WkA = np.ascontiguousarray(f("L0_attn_b_W_key").transpose(1, 0, 2).reshape(D, D))
    WqA = np.ascontiguousarray(f("L0_attn_b_W_query").transpose(1, 0, 2).reshape(D, D))
    WvA = np.ascontiguousarray(f("L0_attn_b_W_values").transpose(1, 0, 2).reshape(D, D))
    Wcat = np.ascontiguousarray(np.concatenate([WkA, WqA, WvA], axis=1))
    bcat = np.concatenate([f("L0_attn_b_Bias_key").reshape(-1),
                           f("L0_attn_b_Bias_query").reshape(-1),
                           f("L0_attn_b_Bias_values").reshape(-1)])
    kqv = run_dense("p3072b", dict(IF=D, OF=3072, ln=True, wmode="res", bias=True),
                    xfm, {"w": Wcat, "b": colvec(bcat)})

    dmask = np.zeros((4, 128, GW), dtype=np.float32)
    for r in range(4):
        for pp in range(128):
            dmask[r, pp, :] = (np.arange(GW) >= pp + 128 * r).astype(np.float32)
    wfm = attn_pass(kqv, masked=True, scale=0.125, dmask=dmask)

    WoA = np.ascontiguousarray(f("L0_attn_b_W_output").reshape(D, D))
    xfm = run_dense("o1024br", dict(IF=D, OF=D, wmode="res", bias=True, resid=True),
                    wfm, {"w": WoA, "b": colvec(f("L0_attn_b_Bias_output")), "res": xfm})

    xfm = l0_ffn(xfm)

    Wc = np.ascontiguousarray(np.concatenate(
        [f("L1_attn_a_Wq").T, f("L1_attn_a_Wk").T, f("L1_attn_a_Wv").T], axis=1))
    qkv1 = run_dense("p3072", dict(IF=D, OF=3072, ln=True, wmode="res"),
                     xfm, {"w": Wc})
    afm = attn_pass(qkv1, masked=False, scale=1.0 / 32.0, dmask=None, order="qkv")
    xfm = run_dense("o1024br", dict(IF=D, OF=D, wmode="res", bias=True, resid=True),
                    afm, {"w": np.ascontiguousarray(f("L1_attn_a_Wu").T),
                          "b": colvec(f("L1_attn_a_bu")), "res": xfm})

    w1b = blk_w(np.ascontiguousarray(f("L1_ffn_w1_w").T), 32)
    w2b = blk_w(np.ascontiguousarray(f("L1_ffn_w2_w").T), 32)
    h1b = blk_w(np.ascontiguousarray(f("L1_ffn_h1_w").T), 32)
    h2b = blk_w(np.ascontiguousarray(f("L1_ffn_h2_w").T), 32)
    w3b = blk_w(np.ascontiguousarray(f("L1_ffn_w3_w").T), 8)
    s1v = colvec(f("L1_ffn_w1_s"))
    s2v = colvec(f("L1_ffn_w2_s"))
    h1s = colvec(f("L1_ffn_h1_s"))
    h2s = colvec(f("L1_ffn_h2_s"))
    w3s = colvec(f("L1_ffn_w3_s"))
    nc_mul4k = cached("mul4096", lambda: build_mul(4096))

    def l1_ffn(xin):
        g1 = run_dense("f1a", dict(IF=D, OF=4096, ln=True, wmode="blk",
                                   actfn="silu", scale_mode="svec"),
                       xin, {"w": w1b, "s": s1v})
        g2 = run_dense("f1b", dict(IF=D, OF=4096, ln=True, wmode="blk",
                                   scale_mode="svec"),
                       xin, {"w": w2b, "s": s2v})
        h = gather_fm(runp(nc_mul4k, [{"a": a, "b": b} for a, b in
                                      zip(shard_fm(g1), shard_fm(g2))]))
        h = run_dense("f1h", dict(IF=4096, OF=4096, wmode="blk",
                                  actfn="gelu", scale_mode="svec"),
                      h, {"w": h1b, "s": h1s})
        h = run_dense("f1h", dict(IF=4096, OF=4096, wmode="blk",
                                  actfn="gelu", scale_mode="svec"),
                      h, {"w": h2b, "s": h2s})
        return run_dense("f1o", dict(IF=4096, OF=D, wmode="blk",
                                     scale_mode="svec", resid=True),
                         h, {"w": w3b, "s": w3s, "res": xin})

    xfm = l1_ffn(xfm)

    selc = np.zeros((128, 2), dtype=np.float32)
    selc[0:64, 0] = 1.0
    selc[64:128, 1] = 1.0
    selr = np.ascontiguousarray(selc.T)
    Wqk = np.ascontiguousarray(np.concatenate([f("L1_attn_b_Wq"), f("L1_attn_b_Wk")], axis=1))
    qk = run_dense("g1qk", dict(IF=D, OF=2048, ln=True, wmode="res", actfn="silu",
                                post_l2n_obs=tuple(range(8, 16))),
                   xfm, {"w": Wqk, "selc": selc, "selr": selr})
    qfm = qk[0:1024]
    kfm = qk[1024:2048]
    vfm = run_dense("g1v", dict(IF=D, OF=D, ln=True, wmode="res"),
                    xfm, {"w": np.ascontiguousarray(f("L1_attn_b_Wv"))})
    zf = run_dense("g1f1", dict(IF=D, OF=64, ln=True, wmode="res"),
                   xfm, {"w": np.ascontiguousarray(f("L1_attn_b_Wf1"))})
    gkfm = run_dense("g1f2", dict(IF=64, OF=D, wmode="res", actfn="logsig"),
                     zf, {"w": np.ascontiguousarray(f("L1_attn_b_Wf2"))})
    gmfm = run_dense("g1gm", dict(IF=D, OF=16, ln=True, wmode="res", actfn="negsig"),
                     xfm, {"w": np.ascontiguousarray(f("L1_attn_b_Wgamma"))})
    og1 = run_dense("g1f1", dict(IF=D, OF=64, ln=True, wmode="res"),
                    xfm, {"w": np.ascontiguousarray(f("L1_attn_b_Wog1"))})
    ogfm = run_dense("g1og2", dict(IF=64, OF=D, wmode="res", actfn="sigmoid"),
                     og1, {"w": np.ascontiguousarray(f("L1_attn_b_Wog2"))})

    triu = np.triu(np.ones((64, 64), dtype=np.float32))
    supm = np.tril(np.ones((64, 64), dtype=np.float32), -1)
    tilm = np.triu(np.ones((64, 64), dtype=np.float32))
    nc_delta = cached("delta", build_delta)
    ofm = np.zeros((D, NTOK), dtype=np.float32)
    for rnd in range(4):
        in_maps = []
        units = []
        for c in range(NCORES):
            un = 4 * c + rnd
            b, h = un // 16, un % 16
            units.append((b, h))
            sl = slice(b * 8192, (b + 1) * 8192)
            qT = np.ascontiguousarray(qfm[h * 64:(h + 1) * 64, sl])
            kT = np.ascontiguousarray(kfm[h * 64:(h + 1) * 64, sl])
            gkT = np.ascontiguousarray(gkfm[h * 64:(h + 1) * 64, sl])
            in_maps.append({
                "qT": qT, "kT": kT, "gkT": gkT,
                "gks": np.ascontiguousarray(gkT.T),
                "vs": np.ascontiguousarray(vfm[h * 64:(h + 1) * 64, sl].T),
                "gc": np.ascontiguousarray(gmfm[h, sl].reshape(8192, 1)),
                "triu": triu, "sup": supm, "til": tilm, "ident": IDENT,
            })
        res = runp(nc_delta, in_maps)
        for c, (b, h) in enumerate(units):
            O = np.asarray(res[c]["o"]).reshape(8192, 64)
            ofm[h * 64:(h + 1) * 64, b * 8192:(b + 1) * 8192] = O.T

    nc_mul1k = cached("mul1024", lambda: build_mul(1024))
    o2 = gather_fm(runp(nc_mul1k, [{"a": a, "b": b} for a, b in
                                   zip(shard_fm(ofm), shard_fm(ogfm))]))
    nw = f("L1_attn_b_norm_w")
    xfm = run_dense("g2", dict(IF=D, OF=D, ln=True, gain=True, wmode="res", resid=True),
                    o2, {"w": np.ascontiguousarray(f("L1_attn_b_Wo")),
                         "gain": np.ascontiguousarray(nw.reshape(8, 128).T), "res": xfm})

    xfm = l1_ffn(xfm)

    return np.ascontiguousarray(xfm.T).reshape(2, 8192, D).astype(np.float32)


def attn_pass(qkv_fm, masked, scale, dmask, order="kqv"):
    key = "attn_m" if masked else "attn_u"
    nc = cached(key, lambda: build_attn(masked, scale))
    if order == "kqv":
        kS, qS, vS = 0, 1024, 2048
    else:
        qS, kS, vS = 0, 1024, 2048
    out = np.zeros((D, NTOK), dtype=np.float32)
    for li in range(2):
        in_maps = []
        units = []
        for c in range(NCORES):
            m = {"ident": IDENT}
            if masked:
                m["dmask"] = dmask
            uu = []
            for j in range(2):
                un = 4 * c + 2 * li + j
                b, n = un // 16, un % 16
                uu.append((b, n))
                sl = slice(b * 8192, (b + 1) * 8192)
                m["q%d" % j] = np.ascontiguousarray(qkv_fm[qS + n * 64:qS + (n + 1) * 64, sl])
                m["k%d" % j] = np.ascontiguousarray(qkv_fm[kS + n * 64:kS + (n + 1) * 64, sl])
                m["v%d" % j] = np.ascontiguousarray(qkv_fm[vS + n * 64:vS + (n + 1) * 64, sl])
            units.append(uu)
            in_maps.append(m)
        res = runp(nc, in_maps)
        for c in range(NCORES):
            for j, (b, n) in enumerate(units[c]):
                out[n * 64:(n + 1) * 64, b * 8192:(b + 1) * 8192] = np.asarray(res[c]["o%d" % j])
    return out
